# version 01
# baseline (speedup 1.0000x reference)
import math
import numpy as np
import ml_dtypes

import concourse.bass as bass
import concourse.mybir as mybir
from concourse.bass_utils import run_bass_kernel_spmd

F32 = mybir.dt.float32
BF16 = mybir.dt.bfloat16
AF = mybir.ActivationFunctionType
ALU = mybir.AluOpType
AX = mybir.AxisListType

D = 2048
HQ, HKV, DH = 16, 4, 128
KVW = HKV * DH
LN_EPS = 1e-5
RMS_EPS = float(np.finfo(np.float32).eps)
SM_SCALE = 1.0 / math.sqrt(DH)

ENGS = ("tensor", "vector", "scalar", "gpsimd", "sync")
CHAIN = ("vector", "scalar")


class KB:

    def __init__(self, nc):
        self.nc = nc
        self.blocks = []
        self.cur = None
        self.sems = {}
        self.cnt = {}
        self.hw = None
        self.block_start = None
        self.dma_sems = set()
        self.last = {}

    def sem(self, name, dma=False):
        if name not in self.sems:
            self.sems[name] = self.nc.alloc_semaphore(name)
            self.cnt[name] = 0
            if dma:
                self.dma_sems.add(name)
        return self.sems[name]

    def block(self):
        self.cur = {e: [] for e in ENGS}
        self.blocks.append(self.cur)
        self.hw = {}
        self.block_start = {s: self.cnt[s] for s in self.cnt if s not in self.dma_sems}

    def emit(self, eng, fn, deps=(), inc=None, dma=False):
        if dma and inc is not None:
            self.sem(inc, dma=True)
            deps = tuple(deps) + ((inc, self.cnt[inc]),)
        if eng in CHAIN:
            deps = tuple(deps) + (self.last.get(eng),)
            assert inc is not None
        if inc is not None:
            self.sem(inc, dma=dma)
        waits = []
        for d in deps:
            if d is None:
                continue
            s, v = d
            self.sem(s)
            if v <= 0 or v <= self.block_start.get(s, 0):
                continue
            k = (eng, s)
            if self.hw.get(k, 0) >= v:
                continue
            self.hw[k] = v
            waits.append((s, v))
        tok = None
        if inc is not None:
            self.cnt[inc] += 16 if dma else 1
            tok = (inc, self.cnt[inc])
            if eng in CHAIN:
                self.last[eng] = tok
        self.cur[eng].append((waits, fn, inc, 16 if dma else 1))
        return tok

    def realize(self):
        for blk in self.blocks:
            with self.nc.Block() as block:
                for eng in ENGS:
                    items = blk[eng]
                    if not items:
                        continue

                    def make_body(items=items):
                        def body(e):
                            for waits, fn, inc, by in items:
                                for (s, v) in waits:
                                    e.wait_ge(self.sems[s], v)
                                ins = fn(e)
                                if inc is not None:
                                    ins.then_inc(self.sems[inc], by)
                        return body

                    getattr(block, eng)(make_body())


def build_nc(T=2048, HF=8192):
    H2 = 2 * HF
    NQT = T // 128
    HQT = NQT // 2
    Th = T // 2
    TBQ = min(4, HQT)
    NTB = HQT // TBQ
    GQ = min(4, HQT)
    NG = HQT // GQ
    QCW = min(512, Th)
    NQC = Th // QCW
    NHT = HF // 128
    NJ = HF // 512

    nc = bass.Bass("TRN2", target_bir_lowering=False)

    x_h = nc.dram_tensor("x", [T, D], F32, kind="ExternalInput")
    wq_h = nc.dram_tensor("wq", [D, D], F32, kind="ExternalInput")
    wk_h = nc.dram_tensor("wk", [D, KVW], F32, kind="ExternalInput")
    wv_h = nc.dram_tensor("wv", [D, KVW], F32, kind="ExternalInput")
    wo_h = nc.dram_tensor("wo", [D, D], F32, kind="ExternalInput")
    wgu_h = nc.dram_tensor("wgu", [H2, D], F32, kind="ExternalInput")
    wpj_h = nc.dram_tensor("wproj", [D, HF], F32, kind="ExternalInput")
    cos_h = nc.dram_tensor("costab", [T, 128], F32, kind="ExternalInput")
    sin_h = nc.dram_tensor("sintab", [T, 64], F32, kind="ExternalInput")
    id_h = nc.dram_tensor("ident", [128, 128], BF16, kind="ExternalInput")
    mk_h = nc.dram_tensor("maskc", [128, 896], BF16, kind="ExternalInput")
    out_h = nc.dram_tensor("out", [T, D], F32, kind="ExternalOutput")

    wtg_h = nc.dram_tensor("wtg", [D, H2], BF16, kind="Internal")
    wtp_h = nc.dram_tensor("wtp", [HF, D], BF16, kind="Internal")
    ffn_h = nc.dram_tensor("ffnb", [T, D], F32, kind="Internal")

    kt = nc.alloc_sbuf_tensor("kt", [128, HKV * T], BF16)
    vb = nc.alloc_sbuf_tensor("vb", [128, NQT * HKV * 132], BF16)
    cos_sb = nc.alloc_sbuf_tensor("cos_sb", [128, NQT * 128], F32)
    sin_sb = nc.alloc_sbuf_tensor("sin_sb", [128, NQT * 64], F32)
    ident = nc.alloc_sbuf_tensor("ident_sb", [128, 128], BF16)
    maskc = nc.alloc_sbuf_tensor("maskc_sb", [128, 896], BF16)
    xnt = nc.alloc_sbuf_tensor("xnt", [128, 16 * Th], BF16)
    big = nc.alloc_sbuf_tensor("big", [128, 32768], BF16)
    wstr = nc.alloc_sbuf_tensor("wstr", [128, 16384], BF16)
    wstr_f = wstr.bitcast(F32)
    xf = nc.alloc_sbuf_tensor("xf", [128, 4096], F32)
    outs = nc.alloc_sbuf_tensor("outs", [128, 2048], F32)
    sl = nc.alloc_sbuf_tensor("sl", [128, 2048], BF16)
    lns = nc.alloc_sbuf_tensor("lns", [128, 16], F32)
    fss = nc.alloc_sbuf_tensor("fss", [128, 16], F32)
    qsf = nc.alloc_sbuf_tensor("qsf", [128, 128], F32)
    qsb = nc.alloc_sbuf_tensor("qsb", [128, 128], BF16)
    qtb = nc.alloc_sbuf_tensor("qtb", [128, 128], BF16)
    rss = nc.alloc_sbuf_tensor("rss", [128, 8], F32)

    ps = nc.alloc_psum_tensor("ps", [128, 4096], F32)

    def bank(i, w=512):
        return ps[:, i * 512:i * 512 + w]

    kb = KB(nc)

    WSEMS = ("wg0", "wg1", "wp0", "wp1", "aw0", "aw1", "wo0", "wo1", "fb0", "fb1")

    def WGRD():
        return tuple((n, kb.cnt.get(n, 0)) for n in WSEMS)

    def XGRD():
        return (("x0", kb.cnt.get("x0", 0)), ("x1", kb.cnt.get("x1", 0)))

    QIN = (0, 8192)
    TSGN, TABS, TM = 16384, 18432, 20480
    STG = 22528
    EOFF = 16384
    QTT = 24576

    kb.block()
    kb.emit("sync", lambda e: e.dma_start(out=cos_sb[:, :], in_=cos_h.ap().rearrange("(q p) c -> p q c", p=128)), inc="ld0", dma=True)
    kb.emit("sync", lambda e: e.dma_start(out=sin_sb[:, :], in_=sin_h.ap().rearrange("(q p) c -> p q c", p=128)), inc="ld0", dma=True)
    kb.emit("sync", lambda e: e.dma_start(out=ident[:, :], in_=id_h.ap()), inc="ld0", dma=True)
    kb.emit("sync", lambda e: e.dma_start(out=maskc[:, :], in_=mk_h.ap()), inc="ld0", dma=True)
    TAB = ("ld0", 64)
    kb.emit("vector", lambda e: e.memset(vb.rearrange("p (q k c) -> p q k c", k=HKV, c=132)[:, :, :, 128:132], 1.0), inc="dv")

    st = {"qgate": None, "ev": {}, "stage": None}

    def quant_chunk(src_ap, ngroups, in_tok):
        v3 = src_ap.rearrange("p (g c) -> p g c", c=64)
        kb.emit("vector", lambda e: e.tensor_reduce(qsf[:, 0:ngroups], v3, AX.X, ALU.add, apply_absolute_value=True), deps=[in_tok], inc="dv")
        kb.emit("vector", lambda e: e.tensor_scalar(qsf[:, 0:ngroups], qsf[:, 0:ngroups], 1.0 / 64.0, 1e-8, ALU.mult, ALU.max), inc="dv")
        kb.emit("vector", lambda e: e.tensor_copy(qsb[:, 0:ngroups], qsf[:, 0:ngroups]), inc="dv")
        t4 = kb.emit("vector", lambda e: e.tensor_scalar_mul(qtb[:, 0:ngroups], qsb[:, 0:ngroups], 0.5), inc="dv")
        n64 = ngroups * 64
        sgn = big[:, TSGN:TSGN + n64]
        ab = big[:, TABS:TABS + n64]
        mm = big[:, TM:TM + n64]
        ta = kb.emit("scalar", lambda e: e.activation(sgn, src_ap, AF.Sign), deps=[in_tok, st["qgate"]], inc="ac")
        tb = kb.emit("scalar", lambda e: e.activation(ab, src_ap, AF.Abs), deps=[in_tok], inc="ac")
        tbv = qtb[:, 0:ngroups][:, :, None].broadcast_to([128, ngroups, 64])
        kb.emit("vector", lambda e: e.tensor_tensor(mm.rearrange("p (g c) -> p g c", c=64), ab.rearrange("p (g c) -> p g c", c=64), tbv, ALU.is_gt), deps=[t4, tb, st["qgate"]], inc="dv")
        kb.emit("vector", lambda e: e.tensor_tensor(mm, mm, sgn, ALU.mult), deps=[ta], inc="dv")
        sbv = qsb[:, 0:ngroups][:, :, None].broadcast_to([128, ngroups, 64])
        te = kb.emit("vector", lambda e: e.tensor_tensor(src_ap.rearrange("p (g c) -> p g c", c=64), mm.rearrange("p (g c) -> p g c", c=64), sbv, ALU.mult), inc="dv")
        st["qgate"] = te
        return te

    def w_transpose_block(src, nt, stage_view, sub_off, tok_ready):
        evs = []
        last_mm = None
        for tt in range(nt):
            pb = 6 + (tt // 4) % 2
            last_mm = kb.emit("tensor", lambda e, tt=tt, pb=pb, src=src: e.matmul(
                bank(pb)[:, (tt % 4) * 128:(tt % 4) * 128 + 128],
                src[:, tt * 128:(tt + 1) * 128], ident[:, :],
                start=True, stop=True),
                deps=[tok_ready, TAB, st["ev"].get(pb)], inc="pe")
            if tt % 4 == 3:
                tt0 = tt - 3
                def ev(e, pb=pb, tt0=tt0, stage_view=stage_view, sub_off=sub_off):
                    o = stage_view(tt0, sub_off)
                    return e.tensor_copy(o, bank(pb).rearrange("p (t c) -> p t c", t=4))
                tv = kb.emit("vector", ev, deps=[last_mm, st["stage"]], inc="dv")
                st["ev"][pb] = tv
                evs.append(tv)
        return evs, last_mm

    in_tok = [None, None]
    cons_tok = [None, None]

    for b in range(H2 // 512):
        slot = b % 2
        qo = QIN[slot]

        def ld(e, b=b, qo=qo):
            return e.dma_start(out=big[:, qo:qo + 4 * D].rearrange("p (s c) -> p s c", s=4),
                               in_=wgu_h.ap()[b * 512:(b + 1) * 512, :].rearrange("(s p) c -> p s c", p=128))
        in_tok[slot] = kb.emit("gpsimd", ld, deps=(cons_tok[slot],), inc="qi%d" % slot, dma=True)
        evall = []
        for sb_i in range(4):
            src = big[:, qo + sb_i * D: qo + (sb_i + 1) * D]
            te = quant_chunk(src, D // 64, in_tok[slot])

            def sv(tt0, sub_off):
                return big[:, STG:STG + 8192].rearrange("p (dt c) -> p dt c", dt=16)[:, tt0:tt0 + 4, sub_off * 128:(sub_off + 1) * 128]
            evs, last_mm = w_transpose_block(src, 16, sv, sb_i, te)
            evall += evs
            cons_tok[slot] = last_mm

        def wr(e, b=b):
            o = wtg_h.ap()[:, b * 512:(b + 1) * 512].rearrange("(dt p) c -> p dt c", p=128)
            return e.dma_start(out=o, in_=big[:, STG:STG + 8192].rearrange("p (dt c) -> p dt c", dt=16))
        st["stage"] = kb.emit("sync", wr, deps=evall, inc="ws", dma=True)

    for b in range(D // 128):
        slot = b % 2
        qo = QIN[slot]

        def ldp(e, b=b, qo=qo):
            return e.dma_start(out=big[:, qo:qo + HF], in_=wpj_h.ap()[b * 128:(b + 1) * 128, :])
        in_tok[slot] = kb.emit("gpsimd", ldp, deps=(cons_tok[slot],), inc="qi%d" % slot, dma=True)
        evall = []
        for sb_i in range(HF // D):
            src = big[:, qo + sb_i * D: qo + (sb_i + 1) * D]
            te = quant_chunk(src, D // 64, in_tok[slot])

            def svp(tt0, sub_off, sb_i=sb_i):
                ht0 = sb_i * 16 + tt0
                return big[:, STG:STG + NHT * 128].rearrange("p (ht c) -> p ht c", ht=NHT)[:, ht0:ht0 + 4, :]
            evs, last_mm = w_transpose_block(src, 16, svp, 0, te)
            evall += evs
            cons_tok[slot] = last_mm

        def wrp(e, b=b):
            o = wtp_h.ap()[:, b * 128:(b + 1) * 128].rearrange("(ht p) c -> p ht c", p=128)
            return e.dma_start(out=o, in_=big[:, STG:STG + NHT * 128].rearrange("p (ht c) -> p ht c", ht=NHT))
        st["stage"] = kb.emit("sync", wrp, deps=evall, inc="ws", dma=True)

    WSD = ("ws", kb.cnt["ws"])

    outs_gate = [None, None]
    xcons = [None, None]
    ffcons = [None, None]

    for half in range(2):
        kb.block()
        st["ev"] = {}
        xtok = [None, None]
        xng = None
        xnt_evs = []
        for lq in range(HQT):
            gq = half * HQT + lq
            slot = lq % 2

            def ldx(e, gq=gq, slot=slot):
                return e.dma_start(out=xf[:, slot * 2048:(slot + 1) * 2048],
                                   in_=x_h.ap()[gq * 128:(gq + 1) * 128, :])
            xtok[slot] = kb.emit("sync", ldx, deps=(xcons[slot],), inc="x%d" % slot, dma=True)
            xs = xf[:, slot * 2048:(slot + 1) * 2048]
            kb.emit("vector", lambda e, xs=xs: e.tensor_reduce(lns[:, 0:1], xs, AX.X, ALU.add), deps=[xtok[slot]], inc="dv")
            tsq = kb.emit("scalar", lambda e, xs=xs: e.activation(big[:, 0:2048], xs, AF.Square, accum_out=lns[:, 1:2]), deps=[xtok[slot], xng], inc="ac")
            kb.emit("vector", lambda e: e.tensor_scalar(lns[:, 2:3], lns[:, 0:1], 1.0 / D, None, ALU.mult), inc="dv")
            kb.emit("vector", lambda e: e.tensor_scalar(lns[:, 3:4], lns[:, 1:2], 1.0 / D, None, ALU.mult), deps=[tsq], inc="dv")
            kb.emit("vector", lambda e: e.tensor_tensor(lns[:, 4:5], lns[:, 2:3], lns[:, 2:3], ALU.mult), inc="dv")
            tm4 = kb.emit("vector", lambda e: e.scalar_tensor_tensor(lns[:, 5:6], lns[:, 3:4], LN_EPS, lns[:, 4:5], ALU.add, ALU.subtract), inc="dv")
            tl1 = kb.emit("scalar", lambda e: e.activation(lns[:, 6:7], lns[:, 5:6], AF.Ln), deps=[tm4], inc="ac")
            tl2 = kb.emit("scalar", lambda e: e.activation(lns[:, 7:8], lns[:, 6:7], AF.Exp, scale=-0.5), inc="ac")
            txn = kb.emit("vector", lambda e, xs=xs: e.tensor_scalar(big[:, 2048:4096], xs, lns[:, 2:3], lns[:, 7:8], ALU.subtract, ALU.mult), deps=[tl2], inc="dv")
            xng = txn
            xcons[slot] = txn
            src = big[:, 2048:4096]

            def svx(tt0, _so, lq=lq):
                return xnt.rearrange("p (dt c) -> p dt c", dt=16)[:, tt0:tt0 + 4, lq * 128:(lq + 1) * 128]
            st["stage"] = None
            evs, _ = w_transpose_block(src, 16, svx, 0, txn)
            xnt_evs += evs
        xnt_done = xnt_evs[-1]

        fstate_junk = None
        for tb in range(NTB):
            kb.block()
            st["ev"] = {}
            q0 = tb * TBQ
            wg_tok = [None, None]
            wg_cons = [None, None]
            ev_gate = {}
            h_tok = {}
            tok = None
            for j in range(NJ):
                for gu in range(2):
                    s2 = (j * 2 + gu) % 2
                    hcol = gu * HF + j * 512

                    def ldw(e, hcol=hcol, s2=s2):
                        return e.dma_start(out=wstr[:, s2 * 8192:(s2 + 1) * 8192].rearrange("p (dt c) -> p dt c", dt=16),
                                           in_=wtg_h.ap()[:, hcol:hcol + 512].rearrange("(dt p) c -> p dt c", p=128))
                    wg_tok[s2] = kb.emit("sync", ldw, deps=(wg_cons[s2], WSD) + WGRD(), inc="wg%d" % s2, dma=True)
                for gu in range(2):
                    s2 = (j * 2 + gu) % 2
                    for dt in range(16):
                        for q in range(TBQ):
                            bnk = q * 2 + gu
                            tok = kb.emit("tensor", lambda e, dt=dt, q=q, bnk=bnk, s2=s2, q0=q0: e.matmul(
                                bank(bnk),
                                xnt[:, dt * Th + (q0 + q) * 128: dt * Th + (q0 + q + 1) * 128],
                                wstr[:, s2 * 8192 + dt * 512: s2 * 8192 + (dt + 1) * 512],
                                start=(dt == 0), stop=(dt == 15)),
                                deps=[wg_tok[s2]] + ([ev_gate.get(bnk)] if dt == 0 else []),
                                inc="pe")
                    wg_cons[s2] = tok
                for q in range(TBQ):
                    tsl = kb.emit("scalar", lambda e, q=q: e.activation(sl[:, q * 512:(q + 1) * 512], bank(q * 2), AF.Silu), deps=[tok, ev_gate.get(("sl", q))], inc="ac")
                    th = kb.emit("vector", lambda e, q=q, j=j, q0=q0: e.tensor_tensor(
                        big[:, q * HF + j * 512: q * HF + (j + 1) * 512],
                        sl[:, q * 512:(q + 1) * 512], bank(q * 2 + 1), ALU.mult), deps=[tsl, tok], inc="dv")
                    ev_gate[q * 2] = tsl
                    ev_gate[q * 2 + 1] = th
                    ev_gate[("sl", q)] = th
                    h_tok[(q, j)] = th
            hnt_done = []
            for q in range(TBQ):
                hq = big[:, q * HF:(q + 1) * HF]
                deps_h = [h_tok[(q, j)] for j in range(NJ)]
                tsq = kb.emit("scalar", lambda e, hq=hq, q=q: e.activation(wstr[:, 8192:8192 + HF], hq, AF.Square, accum_out=fss[:, q:q + 1]),
                              deps=deps_h + [fstate_junk, wg_cons[0], wg_cons[1]], inc="ac")
                fstate_junk = tsq
                tms = kb.emit("vector", lambda e, q=q: e.tensor_scalar(fss[:, 12 + q:13 + q] if False else fss[:, 4 + q:5 + q], fss[:, q:q + 1], 1.0 / HF, RMS_EPS, ALU.mult, ALU.add), deps=[tsq], inc="dv")
                kb.emit("scalar", lambda e, q=q: e.activation(fss[:, 4 + q:5 + q], fss[:, 4 + q:5 + q], AF.Ln), deps=[tms], inc="ac")
                ti = kb.emit("scalar", lambda e, q=q: e.activation(fss[:, 8 + q:9 + q], fss[:, 4 + q:5 + q], AF.Exp, scale=-0.5), inc="ac")
                thn = kb.emit("vector", lambda e, hq=hq, q=q: e.tensor_scalar(hq, hq, fss[:, 8 + q:9 + q], None, ALU.mult), deps=[ti], inc="dv")
                for ht in range(NHT):
                    pb = 6 + (ht // 4) % 2
                    tmm = kb.emit("tensor", lambda e, hq=hq, ht=ht, pb=pb: e.matmul(
                        bank(pb)[:, (ht % 4) * 128:(ht % 4) * 128 + 128],
                        hq[:, ht * 128:(ht + 1) * 128], ident[:, :],
                        start=True, stop=True),
                        deps=[thn, st["ev"].get(pb), ev_gate.get(pb) if pb < 8 else None, ev_gate.get(6), ev_gate.get(7)],
                        inc="pe")
                    if ht % 4 == 3:
                        ht0 = ht - 3
                        def evh(e, pb=pb, ht0=ht0, hq=hq):
                            return e.tensor_copy(hq[:, ht0 * 128:(ht0 + 4) * 128], bank(pb))
                        tv = kb.emit("vector", evh, deps=[tmm], inc="dv")
                        st["ev"][pb] = tv
                        hnt_done.append(tv)
            wp_tok = [None, None]
            wp_cons = [wg_cons[0], fstate_junk]
            pr_gate = {}
            NHG = NHT // 8
            for dc in range(2):
                for hg in range(NHG):
                    s2 = (dc * NHG + hg) % 2

                    def ldp2(e, hg=hg, dc=dc, s2=s2):
                        return e.dma_start(out=wstr[:, s2 * 8192:(s2 + 1) * 8192].rearrange("p (ht c) -> p ht c", ht=8),
                                           in_=wtp_h.ap()[hg * 1024:(hg + 1) * 1024, dc * 1024:(dc + 1) * 1024].rearrange("(ht p) c -> p ht c", p=128))
                    wp_tok[s2] = kb.emit("sync", ldp2, deps=(wp_cons[s2], WSD) + WGRD(), inc="wp%d" % s2, dma=True)
                    for htl in range(8):
                        ht = hg * 8 + htl
                        for q in range(TBQ):
                            for hb in range(2):
                                tok = kb.emit("tensor", lambda e, htl=htl, ht=ht, q=q, hb=hb, s2=s2: e.matmul(
                                    bank(q * 2 + hb),
                                    big[:, q * HF + ht * 128: q * HF + (ht + 1) * 128],
                                    wstr[:, s2 * 8192 + htl * 1024 + hb * 512: s2 * 8192 + htl * 1024 + (hb + 1) * 512],
                                    start=(ht == 0), stop=(ht == NHT - 1)),
                                    deps=[wp_tok[s2]] + (hnt_done + [pr_gate.get(q * 2 + hb), st["ev"].get(q * 2 + hb if q * 2 + hb >= 6 else -1)] if ht == 0 else []),
                                    inc="pe")
                    wp_cons[s2] = tok
                for q in range(TBQ):
                    gq = half * HQT + q0 + q
                    s2 = q % 2
                    o2 = outs[:, s2 * 1024:(s2 + 1) * 1024]
                    tv = kb.emit("vector", lambda e, q=q, o2=o2: e.tensor_copy(o2, ps[:, q * 1024:(q + 1) * 1024]),
                                 deps=[tok, outs_gate[s2]], inc="dv")
                    pr_gate[q * 2] = tv
                    pr_gate[q * 2 + 1] = tv

                    def stf(e, gq=gq, dc=dc, o2=o2):
                        return e.dma_start(out=ffn_h.ap()[gq * 128:(gq + 1) * 128, dc * 1024:(dc + 1) * 1024], in_=o2)
                    outs_gate[s2] = kb.emit("sync", stf, deps=[tv], inc="fs%d" % s2, dma=True)

        kb.block()
        st["ev"] = {}
        aw_tok = [None, None]
        aw_cons = [None, None]
        kv_gate = {}
        for g in range(NG):
            q0 = g * GQ
            rope_gate = {}
            for dqc in range(2):
                for dtb in range(4):
                    s2 = (dqc * 4 + dtb) % 2

                    def ldq(e, dtb=dtb, dqc=dqc, s2=s2):
                        return e.dma_start(out=wstr[:, s2 * 4096:(s2 + 1) * 4096].rearrange("p (dt c) -> p dt c", dt=4),
                                           in_=wq_h.ap()[dtb * 512:(dtb + 1) * 512, dqc * 1024:(dqc + 1) * 1024].rearrange("(dt p) c -> p dt c", p=128))
                    aw_tok[s2] = kb.emit("gpsimd", ldq, deps=(aw_cons[s2],) + WGRD(), inc="aw%d" % s2, dma=True)
                    for dtl in range(4):
                        dt = dtb * 4 + dtl
                        for q in range(GQ):
                            for hb in range(2):
                                tok = kb.emit("tensor", lambda e, dt=dt, dtl=dtl, q=q, hb=hb, s2=s2, q0=q0: e.matmul(
                                    bank(q * 2 + hb),
                                    xnt[:, dt * Th + (q0 + q) * 128: dt * Th + (q0 + q + 1) * 128],
                                    wstr[:, s2 * 4096 + dtl * 1024 + hb * 512: s2 * 4096 + dtl * 1024 + (hb + 1) * 512],
                                    start=(dt == 0), stop=(dt == 15)),
                                    deps=[aw_tok[s2], xnt_done] + ([rope_gate.get(q * 2 + hb), kv_gate.get(("b", q * 2 + hb))] if dt == 0 else []),
                                    inc="pe")
                    aw_cons[s2] = tok
                for q in range(GQ):
                    lq = q0 + q
                    gq = half * HQT + lq
                    pv = ps[:, q * 1024:(q + 1) * 1024].rearrange("p (h x) -> p h x", h=8)
                    cosv = cos_sb[:, gq * 128:(gq + 1) * 128][:, None, :].broadcast_to([128, 8, 128])
                    sinv = sin_sb[:, gq * 64:(gq + 1) * 64][:, None, :].broadcast_to([128, 8, 64])
                    t1v = xf[:, 0:1024].rearrange("p (h x) -> p h x", h=8)
                    t2v = xf[:, 1024:2048].rearrange("p (h x) -> p h x", h=8)
                    qcd = big[:, lq * 2048 + dqc * 1024: lq * 2048 + (dqc + 1) * 1024].rearrange("p (h x) -> p h x", h=8)
                    kb.emit("vector", lambda e, pv=pv, cosv=cosv, t1v=t1v: e.tensor_tensor(t1v, pv, cosv, ALU.mult), deps=[tok, TAB] + list(XGRD()), inc="dv")
                    r2 = kb.emit("vector", lambda e, pv=pv, sinv=sinv, t2v=t2v: e.tensor_tensor(t2v[:, :, 0:64], pv[:, :, 64:128], sinv, ALU.mult), inc="dv")
                    r3 = kb.emit("vector", lambda e, pv=pv, sinv=sinv, t2v=t2v: e.tensor_tensor(t2v[:, :, 64:128], pv[:, :, 0:64], sinv, ALU.mult), inc="dv")
                    kb.emit("vector", lambda e, qcd=qcd, t1v=t1v, t2v=t2v: e.tensor_tensor(qcd[:, :, 0:64], t1v[:, :, 0:64], t2v[:, :, 0:64], ALU.subtract), inc="dv")
                    r5 = kb.emit("vector", lambda e, qcd=qcd, t1v=t1v, t2v=t2v: e.tensor_tensor(qcd[:, :, 64:128], t1v[:, :, 64:128], t2v[:, :, 64:128], ALU.add), inc="dv")
                    rope_gate[q * 2] = r5
                    rope_gate[q * 2 + 1] = r5
            for dtb in range(4):
                s2 = dtb % 2

                def ldk(e, dtb=dtb, s2=s2):
                    return e.dma_start(out=wstr[:, s2 * 4096:(s2 + 1) * 4096].rearrange("p (dt two c) -> p dt two c", dt=4, two=2)[:, :, 0, :],
                                       in_=wk_h.ap()[dtb * 512:(dtb + 1) * 512, :].rearrange("(dt p) c -> p dt c", p=128))

                def ldv(e, dtb=dtb, s2=s2):
                    return e.dma_start(out=wstr[:, s2 * 4096:(s2 + 1) * 4096].rearrange("p (dt two c) -> p dt two c", dt=4, two=2)[:, :, 1, :],
                                       in_=wv_h.ap()[dtb * 512:(dtb + 1) * 512, :].rearrange("(dt p) c -> p dt c", p=128))
                kb.emit("gpsimd", ldk, deps=(aw_cons[s2],) + WGRD(), inc="aw%d" % s2, dma=True)
                aw_tok[s2] = kb.emit("gpsimd", ldv, deps=(), inc="aw%d" % s2, dma=True)
                for dtl in range(4):
                    dt = dtb * 4 + dtl
                    for q in range(GQ):
                        for kvh in range(2):
                            tok = kb.emit("tensor", lambda e, dt=dt, dtl=dtl, q=q, kvh=kvh, s2=s2, q0=q0: e.matmul(
                                bank(q * 2 + kvh),
                                xnt[:, dt * Th + (q0 + q) * 128: dt * Th + (q0 + q + 1) * 128],
                                wstr[:, s2 * 4096 + dtl * 1024 + kvh * 512: s2 * 4096 + dtl * 1024 + (kvh + 1) * 512],
                                start=(dt == 0), stop=(dt == 15)),
                                deps=[aw_tok[s2]] + ([rope_gate.get(q * 2 + kvh), kv_gate.get(("b", q * 2 + kvh))] if dt == 0 else []),
                                inc="pe")
                aw_cons[s2] = tok
            for q in range(GQ):
                lq = q0 + q
                gq = half * HQT + lq
                pkv = bank(q * 2).rearrange("p (h x) -> p h x", h=HKV)
                cosv = cos_sb[:, gq * 128:(gq + 1) * 128][:, None, :].broadcast_to([128, HKV, 128])
                sinv = sin_sb[:, gq * 64:(gq + 1) * 64][:, None, :].broadcast_to([128, HKV, 64])
                t1v = xf[:, 0:512].rearrange("p (h x) -> p h x", h=HKV)
                t2v = xf[:, 1024:1536].rearrange("p (h x) -> p h x", h=HKV)
                kdst = sl[:, 0:512].rearrange("p (h x) -> p h x", h=HKV)
                kb.emit("vector", lambda e, pkv=pkv, cosv=cosv, t1v=t1v: e.tensor_tensor(t1v, pkv, cosv, ALU.mult), deps=[tok] + list(XGRD()), inc="dv")
                kb.emit("vector", lambda e, pkv=pkv, sinv=sinv, t2v=t2v: e.tensor_tensor(t2v[:, :, 0:64], pkv[:, :, 64:128], sinv, ALU.mult), inc="dv")
                r3 = kb.emit("vector", lambda e, pkv=pkv, sinv=sinv, t2v=t2v: e.tensor_tensor(t2v[:, :, 64:128], pkv[:, :, 0:64], sinv, ALU.mult), inc="dv")
                kb.emit("vector", lambda e, kdst=kdst, t1v=t1v, t2v=t2v: e.tensor_tensor(kdst[:, :, 0:64], t1v[:, :, 0:64], t2v[:, :, 0:64], ALU.subtract), inc="dv")
                r5 = kb.emit("vector", lambda e, kdst=kdst, t1v=t1v, t2v=t2v: e.tensor_tensor(kdst[:, :, 64:128], t1v[:, :, 64:128], t2v[:, :, 64:128], ALU.add), inc="dv")
                vdst = vb[:, gq * HKV * 132: (gq + 1) * HKV * 132].rearrange("p (h x) -> p h x", x=132)[:, :, 0:128]
                tvv = kb.emit("scalar", lambda e, vdst=vdst, q=q: e.copy(vdst, bank(q * 2 + 1).rearrange("p (h x) -> p h x", h=HKV)), deps=[tok], inc="ac")
                last_mm = None
                for kv in range(HKV):
                    last_mm = kb.emit("tensor", lambda e, kv=kv, q=q: e.matmul(
                        bank(q * 2)[:, kv * 128:(kv + 1) * 128],
                        sl[:, kv * 128:(kv + 1) * 128], ident[:, :],
                        start=True, stop=True), deps=[r5, r3], inc="pe")
                def evk(e, gq=gq, q=q):
                    o = kt.rearrange("p (k t) -> p k t", k=HKV)[:, :, gq * 128:(gq + 1) * 128]
                    return e.tensor_copy(o, bank(q * 2).rearrange("p (k c) -> p k c", k=HKV))
                tke = kb.emit("vector", evk, deps=[last_mm], inc="dv")
                kv_gate[("b", q * 2)] = tke
                kv_gate[("b", q * 2 + 1)] = tvv

        kb.block()
        st["ev"] = {}
        e_gate = {}
        ctx_gate = {}
        qtt_gate = {}
        scores_last = None
        for head in range(HQ):
            kv = head % HKV
            ev_last = None
            for lq0 in range(0, HQT, 4):
                n = min(4, HQT - lq0)
                pb = 6 + (lq0 // 4) % 2
                last_mm = None
                for i in range(n):
                    lq = lq0 + i
                    last_mm = kb.emit("tensor", lambda e, lq=lq, head=head, pb=pb, i=i: e.matmul(
                        bank(pb)[:, i * 128:(i + 1) * 128],
                        big[:, lq * 2048 + head * 128: lq * 2048 + (head + 1) * 128], ident[:, :],
                        start=True, stop=True),
                        deps=[qtt_gate.get(pb), ctx_gate.get(("c", head - 1, lq)), TAB], inc="pe")

                def evq(e, pb=pb, lq0=lq0, n=n):
                    return e.tensor_copy(big[:, QTT + lq0 * 128: QTT + (lq0 + n) * 128], bank(pb)[:, 0:n * 128])
                ev_last = kb.emit("vector", evq, deps=[last_mm, scores_last], inc="dv")
                qtt_gate[pb] = ev_last
            for qc in range(NQC):
                gqc = half * NQC + qc
                nkt = (gqc + 1) * QCW // 128
                kt0_cross = gqc * QCW // 128
                for ktile in range(nkt):
                    pb = ktile % 2
                    tmm = kb.emit("tensor", lambda e, ktile=ktile, kv=kv, qc=qc, pb=pb: e.matmul(
                        bank(pb, QCW),
                        kt[:, kv * T + ktile * 128: kv * T + (ktile + 1) * 128],
                        big[:, QTT + qc * QCW: QTT + (qc + 1) * QCW],
                        start=True, stop=True),
                        deps=[ev_last, e_gate.get(("x", pb))], inc="pe")
                    scores_last = tmm
                    tex = kb.emit("scalar", lambda e, ktile=ktile, pb=pb: e.activation(
                        big[:, EOFF + ktile * QCW: EOFF + (ktile + 1) * QCW], bank(pb, QCW), AF.Exp, scale=SM_SCALE),
                        deps=[tmm, e_gate.get(("e", ktile))], inc="ac")
                    e_gate[("x", pb)] = tex
                    etok = tex
                    if ktile >= kt0_cross:
                        o_rel = (ktile - kt0_cross) * 128
                        etok = kb.emit("vector", lambda e, ktile=ktile, o_rel=o_rel: e.tensor_tensor(
                            big[:, EOFF + ktile * QCW: EOFF + (ktile + 1) * QCW],
                            big[:, EOFF + ktile * QCW: EOFF + (ktile + 1) * QCW],
                            maskc[:, 384 - o_rel: 384 - o_rel + QCW], ALU.mult), deps=[tex, TAB], inc="dv")
                    e_gate[("m", ktile)] = etok
                for ql in range(QCW // 128):
                    lq = qc * (QCW // 128) + ql
                    gq = half * HQT + lq
                    cb = 4 + ql % 2
                    for ktile in range(gq + 1):
                        tok = kb.emit("tensor", lambda e, ktile=ktile, ql=ql, kv=kv, gq=gq, cb=cb, qc=qc: e.matmul(
                            bank(cb)[:, 0:129],
                            big[:, EOFF + ktile * QCW + ql * 128: EOFF + ktile * QCW + (ql + 1) * 128],
                            vb[:, ktile * HKV * 132 + kv * 132: ktile * HKV * 132 + kv * 132 + 129],
                            start=(ktile == 0), stop=(ktile == gq)),
                            deps=[e_gate.get(("m", ktile))] + ([ctx_gate.get(cb)] if ktile == 0 else []),
                            inc="pe")
                    kb.emit("vector", lambda e, cb=cb, ql=ql: e.reciprocal(rss[:, ql:ql + 1], bank(cb)[:, 128:129]), deps=[tok], inc="dv")
                    tcx = kb.emit("vector", lambda e, cb=cb, ql=ql, lq=lq, head=head: e.tensor_scalar(
                        big[:, lq * 2048 + head * 128: lq * 2048 + (head + 1) * 128],
                        bank(cb)[:, 0:128], rss[:, ql:ql + 1], None, ALU.mult), inc="dv")
                    ctx_gate[cb] = tcx
                    ctx_gate[("c", head, lq)] = tcx
                lastq = qc * (QCW // 128) + (QCW // 128) - 1
                for ktile in range(nkt):
                    e_gate[("e", ktile)] = ctx_gate[("c", head, lastq)]

        kb.block()
        st["ev"] = {}
        wo_tok = [None, None]
        wo_cons = [None, None]
        res_gate = {}
        FSD0 = ("fs0", kb.cnt.get("fs0", 0)); FSD1 = ("fs1", kb.cnt.get("fs1", 0))
        for g in range(NG):
            q0 = g * GQ
            ct_done = []
            for q in range(GQ):
                lq = q0 + q
                for dt in range(16):
                    pb = 6 + (dt // 4) % 2
                    tmm = kb.emit("tensor", lambda e, lq=lq, dt=dt, pb=pb: e.matmul(
                        bank(pb)[:, (dt % 4) * 128:(dt % 4) * 128 + 128],
                        big[:, lq * 2048 + dt * 128: lq * 2048 + (dt + 1) * 128], ident[:, :],
                        start=True, stop=True),
                        deps=[st["ev"].get(pb), ctx_gate.get(("c", HQ - 1, lq)), res_gate.get(6), res_gate.get(7), TAB], inc="pe")
                    if dt % 4 == 3:
                        dt0 = dt - 3

                        def evc(e, pb=pb, dt0=dt0, q=q):
                            return e.tensor_copy(big[:, EOFF + q * 2048 + dt0 * 128: EOFF + q * 2048 + (dt0 + 4) * 128], bank(pb))
                        tv = kb.emit("vector", evc, deps=[tmm, res_gate.get(("cc", q))], inc="dv")
                        st["ev"][pb] = tv
                        ct_done.append(tv)
            xtok = {}
            ff_tok = {}
            for dc in range(2):
                for q in range(GQ):
                    gq = half * HQT + q0 + q

                    def ldx2(e, gq=gq, q=q, dc=dc):
                        return e.dma_start(out=xf[:, q * 1024:(q + 1) * 1024],
                                           in_=x_h.ap()[gq * 128:(gq + 1) * 128, dc * 1024:(dc + 1) * 1024])

                    def ldf(e, gq=gq, q=q, dc=dc):
                        return e.dma_start(out=wstr_f[:, 4096 + q * 1024: 4096 + (q + 1) * 1024],
                                           in_=ffn_h.ap()[gq * 128:(gq + 1) * 128, dc * 1024:(dc + 1) * 1024])
                    xtok[q] = kb.emit("sync", ldx2, deps=(xcons[q % 2], res_gate.get(("xr", q))), inc="x%d" % (q % 2), dma=True)
                    ff_tok[q] = kb.emit("sync", ldf, deps=(ffcons[q % 2], FSD0, FSD1, res_gate.get(("fr", q))) + WGRD(), inc="fb%d" % (q % 2), dma=True)
                for dtb in range(4):
                    s2 = (dc * 4 + dtb) % 2

                    def ldwo(e, dtb=dtb, dc=dc, s2=s2):
                        return e.dma_start(out=wstr[:, s2 * 4096:(s2 + 1) * 4096].rearrange("p (dt c) -> p dt c", dt=4),
                                           in_=wo_h.ap()[dtb * 512:(dtb + 1) * 512, dc * 1024:(dc + 1) * 1024].rearrange("(dt p) c -> p dt c", p=128))
                    wo_tok[s2] = kb.emit("gpsimd", ldwo, deps=(wo_cons[s2],) + WGRD(), inc="wo%d" % s2, dma=True)
                    for dtl in range(4):
                        dt = dtb * 4 + dtl
                        for q in range(GQ):
                            for hb in range(2):
                                tok = kb.emit("tensor", lambda e, dt=dt, dtl=dtl, q=q, hb=hb, s2=s2: e.matmul(
                                    bank(q * 2 + hb),
                                    big[:, EOFF + q * 2048 + dt * 128: EOFF + q * 2048 + (dt + 1) * 128],
                                    wstr[:, s2 * 4096 + dtl * 1024 + hb * 512: s2 * 4096 + dtl * 1024 + (hb + 1) * 512],
                                    start=(dt == 0), stop=(dt == 15)),
                                    deps=[wo_tok[s2]] + (ct_done + [res_gate.get(q * 2 + hb)] if dt == 0 else []),
                                    inc="pe")
                    wo_cons[s2] = tok
                for q in range(GQ):
                    gq = half * HQT + q0 + q
                    s2 = q % 2
                    o2 = outs[:, s2 * 1024:(s2 + 1) * 1024]
                    ta = kb.emit("vector", lambda e, q=q, o2=o2: e.tensor_tensor(
                        o2, ps[:, q * 1024:(q + 1) * 1024],
                        xf[:, q * 1024:(q + 1) * 1024], ALU.add),
                        deps=[tok, xtok[q], outs_gate[s2]], inc="dv")
                    tb2 = kb.emit("vector", lambda e, q=q, o2=o2: e.tensor_tensor(
                        o2, o2, wstr_f[:, 4096 + q * 1024: 4096 + (q + 1) * 1024], ALU.add),
                        deps=[ff_tok[q]], inc="dv")
                    res_gate[q * 2] = tb2
                    res_gate[q * 2 + 1] = tb2
                    res_gate[("xr", q)] = ta
                    res_gate[("fr", q)] = tb2

                    def sto(e, gq=gq, dc=dc, o2=o2):
                        return e.dma_start(out=out_h.ap()[gq * 128:(gq + 1) * 128, dc * 1024:(dc + 1) * 1024], in_=o2)
                    outs_gate[s2] = kb.emit("sync", sto, deps=[tb2], inc="os%d" % s2, dma=True)
                    if dc == 1:
                        xcons[s2] = tb2
                        ffcons[s2] = tb2
                        res_gate[("cc", q)] = tb2

    kb.block()
    kb.emit("sync", lambda e: e.nop(), deps=[("os0", kb.cnt["os0"]), ("os1", kb.cnt.get("os1", 0)), ("fs0", kb.cnt.get("fs0", 0)), ("fs1", kb.cnt.get("fs1", 0)), WSD])

    kb.realize()
    return nc


_NC_CACHE = {}


def _get_nc(T, HF):
    key = (T, HF)
    if key not in _NC_CACHE:
        _NC_CACHE[key] = build_nc(T, HF)
    return _NC_CACHE[key]


def _host_tables(T):
    bf = ml_dtypes.bfloat16
    inv = 1.0 / (10000.0 ** (np.arange(0, DH, 2, dtype=np.float64) / DH))
    ang = np.arange(T, dtype=np.float64)[:, None] * inv[None, :]
    cos = np.cos(ang).astype(np.float32)
    sin = np.sin(ang).astype(np.float32)
    costab = np.ascontiguousarray(np.concatenate([cos, cos], axis=1))
    sintab = np.ascontiguousarray(sin)
    ident = np.eye(128, dtype=bf)
    k_i = np.arange(128)[:, None]
    t_i = np.arange(896)[None, :]
    maskc = ((t_i - 384) >= k_i).astype(bf)
    return costab, sintab, ident, maskc


def _deinterleave_cols(w, block=128):
    d_out = w.shape[1]
    perm = []
    for b0 in range(0, d_out, block):
        perm.extend(range(b0, b0 + block, 2))
        perm.extend(range(b0 + 1, b0 + block, 2))
    return np.ascontiguousarray(w[:, perm])


def _permute_wo_rows(wo):
    rows = []
    for h in range(HQ):
        src = 4 * (h % 4) + h // 4
        rows.extend(range(src * 128, (src + 1) * 128))
    return np.ascontiguousarray(wo[rows, :])


def kernel(L0_attn_x, L0_attn_Wq, L0_attn_Wk, L0_attn_Wv, L0_attn_Wo,
           L0_ffn_w_gate_up, L0_ffn_w_proj):
    x = np.ascontiguousarray(np.asarray(L0_attn_x, dtype=np.float32))
    wq = _deinterleave_cols(np.asarray(L0_attn_Wq, dtype=np.float32))
    wk = _deinterleave_cols(np.asarray(L0_attn_Wk, dtype=np.float32))
    wv = np.ascontiguousarray(np.asarray(L0_attn_Wv, dtype=np.float32))
    wo = _permute_wo_rows(np.asarray(L0_attn_Wo, dtype=np.float32))
    wgu = np.ascontiguousarray(np.asarray(L0_ffn_w_gate_up, dtype=np.float32))
    wpj = np.ascontiguousarray(np.asarray(L0_ffn_w_proj, dtype=np.float32))
    B, T, _ = x.shape
    HF = wpj.shape[1]
    costab, sintab, ident, maskc = _host_tables(T)
    nc = _get_nc(T, HF)
    in_maps = [dict(x=x[b], wq=wq, wk=wk, wv=wv, wo=wo, wgu=wgu, wproj=wpj,
                    costab=costab, sintab=sintab, ident=ident, maskc=maskc)
               for b in range(B)]
    res = run_bass_kernel_spmd(nc, in_maps, core_ids=list(range(B)))
    return np.stack([np.asarray(res.results[b]["out"]) for b in range(B)], axis=0)


# version 02
# speedup vs baseline: 1.4213x; 1.4213x over previous
import math
import numpy as np
import ml_dtypes

import concourse.bass as bass
import concourse.mybir as mybir
from concourse.bass_utils import run_bass_kernel_spmd

F32 = mybir.dt.float32
BF16 = mybir.dt.bfloat16
AF = mybir.ActivationFunctionType
ALU = mybir.AluOpType
AX = mybir.AxisListType

D = 2048
HQ, HKV, DH = 16, 4, 128
KVW = HKV * DH
LN_EPS = 1e-5
RMS_EPS = float(np.finfo(np.float32).eps)
SM_SCALE = 1.0 / math.sqrt(DH)

ENGS = ("tensor", "vector", "scalar", "gpsimd", "sync")
CHAIN = ("vector", "scalar")


class KB:

    def __init__(self, nc):
        self.nc = nc
        self.blocks = []
        self.cur = None
        self.sems = {}
        self.cnt = {}
        self.hw = None
        self.block_start = None
        self.dma_sems = set()
        self.last = {}

    def sem(self, name, dma=False):
        if name not in self.sems:
            self.sems[name] = self.nc.alloc_semaphore(name)
            self.cnt[name] = 0
            if dma:
                self.dma_sems.add(name)
        return self.sems[name]

    def block(self):
        self.cur = {e: [] for e in ENGS}
        self.blocks.append(self.cur)
        self.hw = {}
        self.block_start = {s: self.cnt[s] for s in self.cnt if s not in self.dma_sems}

    def emit(self, eng, fn, deps=(), inc=None, dma=False):
        if dma and inc is not None:
            self.sem(inc, dma=True)
            deps = tuple(deps) + ((inc, self.cnt[inc]),)
        if eng in CHAIN:
            deps = tuple(deps) + (self.last.get(eng),)
            assert inc is not None
        if inc is not None:
            self.sem(inc, dma=dma)
        waits = []
        for d in deps:
            if d is None:
                continue
            s, v = d
            self.sem(s)
            if v <= 0 or v <= self.block_start.get(s, 0):
                continue
            k = (eng, s)
            if self.hw.get(k, 0) >= v:
                continue
            self.hw[k] = v
            waits.append((s, v))
        tok = None
        if inc is not None:
            self.cnt[inc] += 16 if dma else 1
            tok = (inc, self.cnt[inc])
            if eng in CHAIN:
                self.last[eng] = tok
        self.cur[eng].append((waits, fn, inc, 16 if dma else 1))
        return tok

    def realize(self):
        for blk in self.blocks:
            with self.nc.Block() as block:
                for eng in ENGS:
                    items = blk[eng]
                    if not items:
                        continue

                    def make_body(items=items):
                        def body(e):
                            for waits, fn, inc, by in items:
                                for (s, v) in waits:
                                    e.wait_ge(self.sems[s], v)
                                ins = fn(e)
                                if inc is not None:
                                    ins.then_inc(self.sems[inc], by)
                        return body

                    getattr(block, eng)(make_body())


def build_nc(T=2048, HF=8192):
    H2 = 2 * HF
    NQT = T // 128
    HQT = NQT // 2
    Th = T // 2
    TBQ = min(4, HQT)
    NTB = HQT // TBQ
    GQ = min(4, HQT)
    NG = HQT // GQ
    QCW = min(512, Th)
    NQC = Th // QCW
    NHT = HF // 128
    NJ = HF // 512

    nc = bass.Bass("TRN2", target_bir_lowering=False)

    x_h = nc.dram_tensor("x", [T, D], F32, kind="ExternalInput")
    wq_h = nc.dram_tensor("wq", [D, D], F32, kind="ExternalInput")
    wk_h = nc.dram_tensor("wk", [D, KVW], F32, kind="ExternalInput")
    wv_h = nc.dram_tensor("wv", [D, KVW], F32, kind="ExternalInput")
    wo_h = nc.dram_tensor("wo", [D, D], F32, kind="ExternalInput")
    SWG = D // 8
    SWP = HF // 8
    wgu_h = nc.dram_tensor("wgu_s", [H2, SWG], F32, kind="ExternalInput")
    wpj_h = nc.dram_tensor("wproj_s", [D, SWP], F32, kind="ExternalInput")
    shg_h = nc.dram_tensor("shg", [SWG, H2], BF16, kind="Internal")
    shp_h = nc.dram_tensor("shp", [SWP, D], BF16, kind="Internal")
    cos_h = nc.dram_tensor("costab", [T, 128], F32, kind="ExternalInput")
    sin_h = nc.dram_tensor("sintab", [T, 64], F32, kind="ExternalInput")
    id_h = nc.dram_tensor("ident", [128, 128], BF16, kind="ExternalInput")
    mk_h = nc.dram_tensor("maskc", [128, 896], BF16, kind="ExternalInput")
    out_h = nc.dram_tensor("out", [T, D], F32, kind="ExternalOutput")

    wtg_h = nc.dram_tensor("wtg", [D, H2], BF16, kind="Internal")
    wtp_h = nc.dram_tensor("wtp", [HF, D], BF16, kind="Internal")
    ffn_h = nc.dram_tensor("ffnb", [T, D], F32, kind="Internal")

    kt = nc.alloc_sbuf_tensor("kt", [128, HKV * T], BF16)
    vb = nc.alloc_sbuf_tensor("vb", [128, NQT * HKV * 132], BF16)
    cos_sb = nc.alloc_sbuf_tensor("cos_sb", [128, NQT * 128], F32)
    sin_sb = nc.alloc_sbuf_tensor("sin_sb", [128, NQT * 64], F32)
    ident = nc.alloc_sbuf_tensor("ident_sb", [128, 128], BF16)
    maskc = nc.alloc_sbuf_tensor("maskc_sb", [128, 896], BF16)
    xnt = nc.alloc_sbuf_tensor("xnt", [128, 16 * Th], BF16)
    big = nc.alloc_sbuf_tensor("big", [128, 32768], BF16)
    wstr = nc.alloc_sbuf_tensor("wstr", [128, 16384], BF16)
    wstr_f = wstr.bitcast(F32)
    xf = nc.alloc_sbuf_tensor("xf", [128, 4096], F32)
    outs = nc.alloc_sbuf_tensor("outs", [128, 2048], F32)
    sl = nc.alloc_sbuf_tensor("sl", [128, 2048], BF16)
    lns = nc.alloc_sbuf_tensor("lns", [128, 16], F32)
    fss = nc.alloc_sbuf_tensor("fss", [128, 16], F32)
    qsf = nc.alloc_sbuf_tensor("qsf", [128, 128], F32)
    qsb = nc.alloc_sbuf_tensor("qsb", [128, 128], BF16)
    qtb = nc.alloc_sbuf_tensor("qtb", [128, 128], BF16)
    rss = nc.alloc_sbuf_tensor("rss", [128, 8], F32)

    ps = nc.alloc_psum_tensor("ps", [128, 4096], F32)

    def bank(i, w=512):
        return ps[:, i * 512:i * 512 + w]

    kb = KB(nc)

    WSEMS = ("wg0", "wg1", "wp0", "wp1", "aw0", "aw1", "wo0", "wo1", "fb0", "fb1")

    def WGRD():
        return tuple((n, kb.cnt.get(n, 0)) for n in WSEMS)

    def XGRD():
        return (("x0", kb.cnt.get("x0", 0)), ("x1", kb.cnt.get("x1", 0)))

    QIN = (0, 8192)
    TSGN, TABS, TM = 16384, 18432, 20480
    STG = 22528
    EOFF = 16384
    QTT = 24576

    kb.block()
    kb.emit("sync", lambda e: e.dma_start(out=cos_sb[:, :], in_=cos_h.ap().rearrange("(q p) c -> p q c", p=128)), inc="ld0", dma=True)
    kb.emit("sync", lambda e: e.dma_start(out=sin_sb[:, :], in_=sin_h.ap().rearrange("(q p) c -> p q c", p=128)), inc="ld0", dma=True)
    kb.emit("sync", lambda e: e.dma_start(out=ident[:, :], in_=id_h.ap()), inc="ld0", dma=True)
    kb.emit("sync", lambda e: e.dma_start(out=maskc[:, :], in_=mk_h.ap()), inc="ld0", dma=True)
    TAB = ("ld0", 64)
    kb.emit("vector", lambda e: e.memset(vb.rearrange("p (q k c) -> p q k c", k=HKV, c=132)[:, :, :, 128:132], 1.0), inc="dv")

    st = {"qgate": None, "ev": {}, "stage": None}

    def quant_chunk(src_ap, ngroups, in_tok):
        v3 = src_ap.rearrange("p (g c) -> p g c", c=64)
        kb.emit("vector", lambda e: e.tensor_reduce(qsf[:, 0:ngroups], v3, AX.X, ALU.add, apply_absolute_value=True), deps=[in_tok], inc="dv")
        kb.emit("vector", lambda e: e.tensor_scalar(qsf[:, 0:ngroups], qsf[:, 0:ngroups], 1.0 / 64.0, 1e-8, ALU.mult, ALU.max), inc="dv")
        kb.emit("vector", lambda e: e.tensor_copy(qsb[:, 0:ngroups], qsf[:, 0:ngroups]), inc="dv")
        t4 = kb.emit("vector", lambda e: e.tensor_scalar_mul(qtb[:, 0:ngroups], qsb[:, 0:ngroups], 0.5), inc="dv")
        n64 = ngroups * 64
        sgn = big[:, TSGN:TSGN + n64]
        ab = big[:, TABS:TABS + n64]
        mm = big[:, TM:TM + n64]
        ta = kb.emit("scalar", lambda e: e.activation(sgn, src_ap, AF.Sign), deps=[in_tok, st["qgate"]], inc="ac")
        tb = kb.emit("scalar", lambda e: e.activation(ab, src_ap, AF.Abs), deps=[in_tok], inc="ac")
        tbv = qtb[:, 0:ngroups][:, :, None].broadcast_to([128, ngroups, 64])
        kb.emit("vector", lambda e: e.tensor_tensor(mm.rearrange("p (g c) -> p g c", c=64), ab.rearrange("p (g c) -> p g c", c=64), tbv, ALU.is_gt), deps=[t4, tb, st["qgate"]], inc="dv")
        kb.emit("vector", lambda e: e.tensor_tensor(mm, mm, sgn, ALU.mult), deps=[ta], inc="dv")
        sbv = qsb[:, 0:ngroups][:, :, None].broadcast_to([128, ngroups, 64])
        te = kb.emit("vector", lambda e: e.tensor_tensor(src_ap.rearrange("p (g c) -> p g c", c=64), mm.rearrange("p (g c) -> p g c", c=64), sbv, ALU.mult), inc="dv")
        st["qgate"] = te
        return te

    def w_transpose_block(src, nt, stage_view, sub_off, tok_ready):
        evs = []
        last_mm = None
        for tt in range(nt):
            pb = 6 + (tt // 4) % 2
            last_mm = kb.emit("tensor", lambda e, tt=tt, pb=pb, src=src: e.matmul(
                bank(pb)[:, (tt % 4) * 128:(tt % 4) * 128 + 128],
                src[:, tt * 128:(tt + 1) * 128], ident[:, :],
                start=True, stop=True),
                deps=[tok_ready, TAB, st["ev"].get(pb)], inc="pe")
            if tt % 4 == 3:
                tt0 = tt - 3
                def ev(e, pb=pb, tt0=tt0, stage_view=stage_view, sub_off=sub_off):
                    o = stage_view(tt0, sub_off)
                    return e.tensor_copy(o, bank(pb).rearrange("p (t c) -> p t c", t=4))
                tv = kb.emit("vector", ev, deps=[last_mm, st["stage"]], inc="dv")
                st["ev"][pb] = tv
                evs.append(tv)
        return evs, last_mm

    in_tok = [None, None]
    cons_tok = [None, None]

    def w_phase_tile_transposes(src, K, tok_ready):
        evs = []
        last_mm = None
        nS = 16 // K
        stageR = big[:, STG:STG + 2048].rearrange("p (k sc) -> p k sc", k=K)
        for tt in range(16):
            pb = 6 + (tt // 4) % 2
            last_mm = kb.emit("tensor", lambda e, tt=tt, pb=pb, src=src: e.matmul(
                bank(pb)[:, (tt % 4) * 128:(tt % 4) * 128 + 128],
                src[:, tt * 128:(tt + 1) * 128], ident[:, :],
                start=True, stop=True),
                deps=[tok_ready, TAB, st["ev"].get(pb)], inc="pe")
            if tt % 4 == 3:
                tt0 = tt - 3
                if K >= 4:
                    s0, k0 = tt0 // K, tt0 % K
                    o = stageR[:, k0:k0 + 4, s0 * 128:(s0 + 1) * 128]
                    i = bank(pb).rearrange("p (t c) -> p t c", t=4)
                else:
                    s0 = tt0 // K
                    o = big[:, STG:STG + 2048].rearrange("p (k s c) -> p s k c", k=K, s=nS)[:, s0:s0 + 2, :, :]
                    i = bank(pb).rearrange("p (s k c) -> p s k c", s=2, k=2)
                def ev(e, o=o, i=i):
                    return e.tensor_copy(o, i)
                tv = kb.emit("vector", ev, deps=[last_mm, st["stage"]], inc="dv")
                st["ev"][pb] = tv
                evs.append(tv)
        return evs, last_mm

    nsg = SWG // 128
    for b in range(H2 // 1024):
        slot = b % 2
        qo = QIN[slot]

        def ld(e, b=b, qo=qo):
            return e.dma_start(out=big[:, qo:qo + 2048].rearrange("p (s c) -> p s c", c=SWG),
                               in_=wgu_h.ap()[b * 1024:(b + 1) * 1024, :].rearrange("(s p) c -> p s c", p=128))
        in_tok[slot] = kb.emit("gpsimd", ld, deps=(cons_tok[slot],), inc="qi%d" % slot, dma=True)
        src = big[:, qo:qo + 2048]
        te = quant_chunk(src, 32, in_tok[slot])
        evall, last_mm = w_phase_tile_transposes(src, nsg, te)
        cons_tok[slot] = last_mm

        def wrg(e, b=b):
            o = shg_h.ap()[:, b * 1024:(b + 1) * 1024].rearrange("(k p) h -> p k h", p=128)
            return e.dma_start(out=o, in_=big[:, STG:STG + 2048].rearrange("p (k sc) -> p k sc", k=nsg))
        st["stage"] = kb.emit("sync", wrg, deps=evall, inc="ws", dma=True)

    nbp = 2048 // SWP
    nsp = SWP // 128
    for b in range(16 // nbp):
        slot = b % 2
        qo = QIN[slot]

        def ldp(e, b=b, qo=qo):
            return e.dma_start(out=big[:, qo:qo + 2048].rearrange("p (s c) -> p s c", c=SWP),
                               in_=wpj_h.ap()[b * nbp * 128:(b + 1) * nbp * 128, :].rearrange("(s p) c -> p s c", p=128))
        in_tok[slot] = kb.emit("gpsimd", ldp, deps=(cons_tok[slot],), inc="qi%d" % slot, dma=True)
        src = big[:, qo:qo + 2048]
        te = quant_chunk(src, 32, in_tok[slot])
        evall, last_mm = w_phase_tile_transposes(src, nsp, te)
        cons_tok[slot] = last_mm

        def wrp(e, b=b):
            o = shp_h.ap()[:, b * nbp * 128:(b + 1) * nbp * 128].rearrange("(k p) c -> p k c", p=128)
            return e.dma_start(out=o, in_=big[:, STG:STG + 2048].rearrange("p (k sc) -> p k sc", k=nsp))
        st["stage"] = kb.emit("sync", wrp, deps=evall, inc="ws", dma=True)

    RG = [[0, 1, 2, 3, 4, 5, 6, 7]]
    kb.sem("cc")
    kb.dma_sems.add("cc")
    kb.emit("gpsimd", lambda e: e.collective_compute("AllGather", ALU.bypass, RG, [shg_h.ap().opt()], [wtg_h.ap().opt()]), deps=[("ws", kb.cnt["ws"])], inc="cc")
    kb.emit("gpsimd", lambda e: e.collective_compute("AllGather", ALU.bypass, RG, [shp_h.ap().opt()], [wtp_h.ap().opt()]), inc="cc")

    WSD = ("cc", kb.cnt["cc"])

    outs_gate = [None, None]
    xcons = [None, None]
    ffcons = [None, None]

    for half in range(2):
        kb.block()
        st["ev"] = {}
        xtok = [None, None]
        xng = None
        xnt_evs = []
        for lq in range(HQT):
            gq = half * HQT + lq
            slot = lq % 2

            def ldx(e, gq=gq, slot=slot):
                return e.dma_start(out=xf[:, slot * 2048:(slot + 1) * 2048],
                                   in_=x_h.ap()[gq * 128:(gq + 1) * 128, :])
            xtok[slot] = kb.emit("sync", ldx, deps=(xcons[slot],), inc="x%d" % slot, dma=True)
            xs = xf[:, slot * 2048:(slot + 1) * 2048]
            kb.emit("vector", lambda e, xs=xs: e.tensor_reduce(lns[:, 0:1], xs, AX.X, ALU.add), deps=[xtok[slot]], inc="dv")
            tsq = kb.emit("scalar", lambda e, xs=xs: e.activation(big[:, 0:2048], xs, AF.Square, accum_out=lns[:, 1:2]), deps=[xtok[slot], xng], inc="ac")
            kb.emit("vector", lambda e: e.tensor_scalar(lns[:, 2:3], lns[:, 0:1], 1.0 / D, None, ALU.mult), inc="dv")
            kb.emit("vector", lambda e: e.tensor_scalar(lns[:, 3:4], lns[:, 1:2], 1.0 / D, None, ALU.mult), deps=[tsq], inc="dv")
            kb.emit("vector", lambda e: e.tensor_tensor(lns[:, 4:5], lns[:, 2:3], lns[:, 2:3], ALU.mult), inc="dv")
            tm4 = kb.emit("vector", lambda e: e.scalar_tensor_tensor(lns[:, 5:6], lns[:, 3:4], LN_EPS, lns[:, 4:5], ALU.add, ALU.subtract), inc="dv")
            tl1 = kb.emit("scalar", lambda e: e.activation(lns[:, 6:7], lns[:, 5:6], AF.Ln), deps=[tm4], inc="ac")
            tl2 = kb.emit("scalar", lambda e: e.activation(lns[:, 7:8], lns[:, 6:7], AF.Exp, scale=-0.5), inc="ac")
            txn = kb.emit("vector", lambda e, xs=xs: e.tensor_scalar(big[:, 2048:4096], xs, lns[:, 2:3], lns[:, 7:8], ALU.subtract, ALU.mult), deps=[tl2], inc="dv")
            xng = txn
            xcons[slot] = txn
            src = big[:, 2048:4096]

            def svx(tt0, _so, lq=lq):
                return xnt.rearrange("p (dt c) -> p dt c", dt=16)[:, tt0:tt0 + 4, lq * 128:(lq + 1) * 128]
            st["stage"] = None
            evs, _ = w_transpose_block(src, 16, svx, 0, txn)
            xnt_evs += evs
        xnt_done = xnt_evs[-1]

        fstate_junk = None
        for tb in range(NTB):
            kb.block()
            st["ev"] = {}
            q0 = tb * TBQ
            wg_tok = [None, None]
            wg_cons = [None, None]
            ev_gate = {}
            h_tok = {}
            tok = None
            for j in range(NJ):
                for gu in range(2):
                    s2 = (j * 2 + gu) % 2
                    hcol = gu * HF + j * 512

                    def ldw(e, hcol=hcol, s2=s2):
                        return e.dma_start(out=wstr[:, s2 * 8192:(s2 + 1) * 8192].rearrange("p (dt c) -> p dt c", dt=16),
                                           in_=wtg_h.ap()[:, hcol:hcol + 512].rearrange("(dt p) c -> p dt c", p=128))
                    wg_tok[s2] = kb.emit("sync", ldw, deps=(wg_cons[s2], WSD) + WGRD(), inc="wg%d" % s2, dma=True)
                for gu in range(2):
                    s2 = (j * 2 + gu) % 2
                    for dt in range(16):
                        for q in range(TBQ):
                            bnk = q * 2 + gu
                            tok = kb.emit("tensor", lambda e, dt=dt, q=q, bnk=bnk, s2=s2, q0=q0: e.matmul(
                                bank(bnk),
                                xnt[:, dt * Th + (q0 + q) * 128: dt * Th + (q0 + q + 1) * 128],
                                wstr[:, s2 * 8192 + dt * 512: s2 * 8192 + (dt + 1) * 512],
                                start=(dt == 0), stop=(dt == 15)),
                                deps=[wg_tok[s2]] + ([ev_gate.get(bnk)] if dt == 0 else []),
                                inc="pe")
                    wg_cons[s2] = tok
                for q in range(TBQ):
                    tsl = kb.emit("scalar", lambda e, q=q: e.activation(sl[:, q * 512:(q + 1) * 512], bank(q * 2), AF.Silu), deps=[tok, ev_gate.get(("sl", q))], inc="ac")
                    th = kb.emit("vector", lambda e, q=q, j=j, q0=q0: e.tensor_tensor(
                        big[:, q * HF + j * 512: q * HF + (j + 1) * 512],
                        sl[:, q * 512:(q + 1) * 512], bank(q * 2 + 1), ALU.mult), deps=[tsl, tok], inc="dv")
                    ev_gate[q * 2] = tsl
                    ev_gate[q * 2 + 1] = th
                    ev_gate[("sl", q)] = th
                    h_tok[(q, j)] = th
            hnt_done = []
            for q in range(TBQ):
                hq = big[:, q * HF:(q + 1) * HF]
                deps_h = [h_tok[(q, j)] for j in range(NJ)]
                tsq = kb.emit("scalar", lambda e, hq=hq, q=q: e.activation(wstr[:, 8192:8192 + HF], hq, AF.Square, accum_out=fss[:, q:q + 1]),
                              deps=deps_h + [fstate_junk, wg_cons[0], wg_cons[1]], inc="ac")
                fstate_junk = tsq
                tms = kb.emit("vector", lambda e, q=q: e.tensor_scalar(fss[:, 12 + q:13 + q] if False else fss[:, 4 + q:5 + q], fss[:, q:q + 1], 1.0 / HF, RMS_EPS, ALU.mult, ALU.add), deps=[tsq], inc="dv")
                kb.emit("scalar", lambda e, q=q: e.activation(fss[:, 4 + q:5 + q], fss[:, 4 + q:5 + q], AF.Ln), deps=[tms], inc="ac")
                ti = kb.emit("scalar", lambda e, q=q: e.activation(fss[:, 8 + q:9 + q], fss[:, 4 + q:5 + q], AF.Exp, scale=-0.5), inc="ac")
                thn = kb.emit("vector", lambda e, hq=hq, q=q: e.tensor_scalar(hq, hq, fss[:, 8 + q:9 + q], None, ALU.mult), deps=[ti], inc="dv")
                for ht in range(NHT):
                    pb = 6 + (ht // 4) % 2
                    tmm = kb.emit("tensor", lambda e, hq=hq, ht=ht, pb=pb: e.matmul(
                        bank(pb)[:, (ht % 4) * 128:(ht % 4) * 128 + 128],
                        hq[:, ht * 128:(ht + 1) * 128], ident[:, :],
                        start=True, stop=True),
                        deps=[thn, st["ev"].get(pb), ev_gate.get(pb) if pb < 8 else None, ev_gate.get(6), ev_gate.get(7)],
                        inc="pe")
                    if ht % 4 == 3:
                        ht0 = ht - 3
                        def evh(e, pb=pb, ht0=ht0, hq=hq):
                            return e.tensor_copy(hq[:, ht0 * 128:(ht0 + 4) * 128], bank(pb))
                        tv = kb.emit("vector", evh, deps=[tmm], inc="dv")
                        st["ev"][pb] = tv
                        hnt_done.append(tv)
            wp_tok = [None, None]
            wp_cons = [wg_cons[0], fstate_junk]
            pr_gate = {}
            NHG = NHT // 8
            for dc in range(2):
                for hg in range(NHG):
                    s2 = (dc * NHG + hg) % 2

                    def ldp2(e, hg=hg, dc=dc, s2=s2):
                        return e.dma_start(out=wstr[:, s2 * 8192:(s2 + 1) * 8192].rearrange("p (ht c) -> p ht c", ht=8),
                                           in_=wtp_h.ap()[hg * 1024:(hg + 1) * 1024, dc * 1024:(dc + 1) * 1024].rearrange("(ht p) c -> p ht c", p=128))
                    wp_tok[s2] = kb.emit("sync", ldp2, deps=(wp_cons[s2], WSD) + WGRD(), inc="wp%d" % s2, dma=True)
                    for htl in range(8):
                        ht = hg * 8 + htl
                        for q in range(TBQ):
                            for hb in range(2):
                                tok = kb.emit("tensor", lambda e, htl=htl, ht=ht, q=q, hb=hb, s2=s2: e.matmul(
                                    bank(q * 2 + hb),
                                    big[:, q * HF + ht * 128: q * HF + (ht + 1) * 128],
                                    wstr[:, s2 * 8192 + htl * 1024 + hb * 512: s2 * 8192 + htl * 1024 + (hb + 1) * 512],
                                    start=(ht == 0), stop=(ht == NHT - 1)),
                                    deps=[wp_tok[s2]] + (hnt_done + [pr_gate.get(q * 2 + hb), st["ev"].get(q * 2 + hb if q * 2 + hb >= 6 else -1)] if ht == 0 else []),
                                    inc="pe")
                    wp_cons[s2] = tok
                for q in range(TBQ):
                    gq = half * HQT + q0 + q
                    s2 = q % 2
                    o2 = outs[:, s2 * 1024:(s2 + 1) * 1024]
                    tv = kb.emit("vector", lambda e, q=q, o2=o2: e.tensor_copy(o2, ps[:, q * 1024:(q + 1) * 1024]),
                                 deps=[tok, outs_gate[s2]], inc="dv")
                    pr_gate[q * 2] = tv
                    pr_gate[q * 2 + 1] = tv

                    def stf(e, gq=gq, dc=dc, o2=o2):
                        return e.dma_start(out=ffn_h.ap()[gq * 128:(gq + 1) * 128, dc * 1024:(dc + 1) * 1024], in_=o2)
                    outs_gate[s2] = kb.emit("sync", stf, deps=[tv], inc="fs%d" % s2, dma=True)

        kb.block()
        st["ev"] = {}
        aw_tok = [None, None]
        aw_cons = [None, None]
        kv_gate = {}
        for g in range(NG):
            q0 = g * GQ
            rope_gate = {}
            for dqc in range(2):
                for dtb in range(4):
                    s2 = (dqc * 4 + dtb) % 2

                    def ldq(e, dtb=dtb, dqc=dqc, s2=s2):
                        return e.dma_start(out=wstr[:, s2 * 4096:(s2 + 1) * 4096].rearrange("p (dt c) -> p dt c", dt=4),
                                           in_=wq_h.ap()[dtb * 512:(dtb + 1) * 512, dqc * 1024:(dqc + 1) * 1024].rearrange("(dt p) c -> p dt c", p=128))
                    aw_tok[s2] = kb.emit("gpsimd", ldq, deps=(aw_cons[s2],) + WGRD(), inc="aw%d" % s2, dma=True)
                    for dtl in range(4):
                        dt = dtb * 4 + dtl
                        for q in range(GQ):
                            for hb in range(2):
                                tok = kb.emit("tensor", lambda e, dt=dt, dtl=dtl, q=q, hb=hb, s2=s2, q0=q0: e.matmul(
                                    bank(q * 2 + hb),
                                    xnt[:, dt * Th + (q0 + q) * 128: dt * Th + (q0 + q + 1) * 128],
                                    wstr[:, s2 * 4096 + dtl * 1024 + hb * 512: s2 * 4096 + dtl * 1024 + (hb + 1) * 512],
                                    start=(dt == 0), stop=(dt == 15)),
                                    deps=[aw_tok[s2], xnt_done] + ([rope_gate.get(q * 2 + hb), kv_gate.get(("b", q * 2 + hb))] if dt == 0 else []),
                                    inc="pe")
                    aw_cons[s2] = tok
                for q in range(GQ):
                    lq = q0 + q
                    gq = half * HQT + lq
                    pv = ps[:, q * 1024:(q + 1) * 1024].rearrange("p (h x) -> p h x", h=8)
                    cosv = cos_sb[:, gq * 128:(gq + 1) * 128][:, None, :].broadcast_to([128, 8, 128])
                    sinv = sin_sb[:, gq * 64:(gq + 1) * 64][:, None, :].broadcast_to([128, 8, 64])
                    t1v = xf[:, 0:1024].rearrange("p (h x) -> p h x", h=8)
                    t2v = xf[:, 1024:2048].rearrange("p (h x) -> p h x", h=8)
                    qcd = big[:, lq * 2048 + dqc * 1024: lq * 2048 + (dqc + 1) * 1024].rearrange("p (h x) -> p h x", h=8)
                    kb.emit("vector", lambda e, pv=pv, cosv=cosv, t1v=t1v: e.tensor_tensor(t1v, pv, cosv, ALU.mult), deps=[tok, TAB] + list(XGRD()), inc="dv")
                    r2 = kb.emit("vector", lambda e, pv=pv, sinv=sinv, t2v=t2v: e.tensor_tensor(t2v[:, :, 0:64], pv[:, :, 64:128], sinv, ALU.mult), inc="dv")
                    r3 = kb.emit("vector", lambda e, pv=pv, sinv=sinv, t2v=t2v: e.tensor_tensor(t2v[:, :, 64:128], pv[:, :, 0:64], sinv, ALU.mult), inc="dv")
                    kb.emit("vector", lambda e, qcd=qcd, t1v=t1v, t2v=t2v: e.tensor_tensor(qcd[:, :, 0:64], t1v[:, :, 0:64], t2v[:, :, 0:64], ALU.subtract), inc="dv")
                    r5 = kb.emit("vector", lambda e, qcd=qcd, t1v=t1v, t2v=t2v: e.tensor_tensor(qcd[:, :, 64:128], t1v[:, :, 64:128], t2v[:, :, 64:128], ALU.add), inc="dv")
                    rope_gate[q * 2] = r5
                    rope_gate[q * 2 + 1] = r5
            for dtb in range(4):
                s2 = dtb % 2

                def ldk(e, dtb=dtb, s2=s2):
                    return e.dma_start(out=wstr[:, s2 * 4096:(s2 + 1) * 4096].rearrange("p (dt two c) -> p dt two c", dt=4, two=2)[:, :, 0, :],
                                       in_=wk_h.ap()[dtb * 512:(dtb + 1) * 512, :].rearrange("(dt p) c -> p dt c", p=128))

                def ldv(e, dtb=dtb, s2=s2):
                    return e.dma_start(out=wstr[:, s2 * 4096:(s2 + 1) * 4096].rearrange("p (dt two c) -> p dt two c", dt=4, two=2)[:, :, 1, :],
                                       in_=wv_h.ap()[dtb * 512:(dtb + 1) * 512, :].rearrange("(dt p) c -> p dt c", p=128))
                kb.emit("gpsimd", ldk, deps=(aw_cons[s2],) + WGRD(), inc="aw%d" % s2, dma=True)
                aw_tok[s2] = kb.emit("gpsimd", ldv, deps=(), inc="aw%d" % s2, dma=True)
                for dtl in range(4):
                    dt = dtb * 4 + dtl
                    for q in range(GQ):
                        for kvh in range(2):
                            tok = kb.emit("tensor", lambda e, dt=dt, dtl=dtl, q=q, kvh=kvh, s2=s2, q0=q0: e.matmul(
                                bank(q * 2 + kvh),
                                xnt[:, dt * Th + (q0 + q) * 128: dt * Th + (q0 + q + 1) * 128],
                                wstr[:, s2 * 4096 + dtl * 1024 + kvh * 512: s2 * 4096 + dtl * 1024 + (kvh + 1) * 512],
                                start=(dt == 0), stop=(dt == 15)),
                                deps=[aw_tok[s2]] + ([rope_gate.get(q * 2 + kvh), kv_gate.get(("b", q * 2 + kvh))] if dt == 0 else []),
                                inc="pe")
                aw_cons[s2] = tok
            for q in range(GQ):
                lq = q0 + q
                gq = half * HQT + lq
                pkv = bank(q * 2).rearrange("p (h x) -> p h x", h=HKV)
                cosv = cos_sb[:, gq * 128:(gq + 1) * 128][:, None, :].broadcast_to([128, HKV, 128])
                sinv = sin_sb[:, gq * 64:(gq + 1) * 64][:, None, :].broadcast_to([128, HKV, 64])
                t1v = xf[:, 0:512].rearrange("p (h x) -> p h x", h=HKV)
                t2v = xf[:, 1024:1536].rearrange("p (h x) -> p h x", h=HKV)
                kdst = sl[:, 0:512].rearrange("p (h x) -> p h x", h=HKV)
                kb.emit("vector", lambda e, pkv=pkv, cosv=cosv, t1v=t1v: e.tensor_tensor(t1v, pkv, cosv, ALU.mult), deps=[tok] + list(XGRD()), inc="dv")
                kb.emit("vector", lambda e, pkv=pkv, sinv=sinv, t2v=t2v: e.tensor_tensor(t2v[:, :, 0:64], pkv[:, :, 64:128], sinv, ALU.mult), inc="dv")
                r3 = kb.emit("vector", lambda e, pkv=pkv, sinv=sinv, t2v=t2v: e.tensor_tensor(t2v[:, :, 64:128], pkv[:, :, 0:64], sinv, ALU.mult), inc="dv")
                kb.emit("vector", lambda e, kdst=kdst, t1v=t1v, t2v=t2v: e.tensor_tensor(kdst[:, :, 0:64], t1v[:, :, 0:64], t2v[:, :, 0:64], ALU.subtract), inc="dv")
                r5 = kb.emit("vector", lambda e, kdst=kdst, t1v=t1v, t2v=t2v: e.tensor_tensor(kdst[:, :, 64:128], t1v[:, :, 64:128], t2v[:, :, 64:128], ALU.add), inc="dv")
                vdst = vb[:, gq * HKV * 132: (gq + 1) * HKV * 132].rearrange("p (h x) -> p h x", x=132)[:, :, 0:128]
                tvv = kb.emit("scalar", lambda e, vdst=vdst, q=q: e.copy(vdst, bank(q * 2 + 1).rearrange("p (h x) -> p h x", h=HKV)), deps=[tok], inc="ac")
                last_mm = None
                for kv in range(HKV):
                    last_mm = kb.emit("tensor", lambda e, kv=kv, q=q: e.matmul(
                        bank(q * 2)[:, kv * 128:(kv + 1) * 128],
                        sl[:, kv * 128:(kv + 1) * 128], ident[:, :],
                        start=True, stop=True), deps=[r5, r3], inc="pe")
                def evk(e, gq=gq, q=q):
                    o = kt.rearrange("p (k t) -> p k t", k=HKV)[:, :, gq * 128:(gq + 1) * 128]
                    return e.tensor_copy(o, bank(q * 2).rearrange("p (k c) -> p k c", k=HKV))
                tke = kb.emit("vector", evk, deps=[last_mm], inc="dv")
                kv_gate[("b", q * 2)] = tke
                kv_gate[("b", q * 2 + 1)] = tvv

        kb.block()
        st["ev"] = {}
        e_gate = {}
        ctx_gate = {}
        qtt_gate = {}
        scores_last = None
        for head in range(HQ):
            kv = head % HKV
            ev_last = None
            for lq0 in range(0, HQT, 4):
                n = min(4, HQT - lq0)
                pb = 6 + (lq0 // 4) % 2
                last_mm = None
                for i in range(n):
                    lq = lq0 + i
                    last_mm = kb.emit("tensor", lambda e, lq=lq, head=head, pb=pb, i=i: e.matmul(
                        bank(pb)[:, i * 128:(i + 1) * 128],
                        big[:, lq * 2048 + head * 128: lq * 2048 + (head + 1) * 128], ident[:, :],
                        start=True, stop=True),
                        deps=[qtt_gate.get(pb), ctx_gate.get(("c", head - 1, lq)), TAB], inc="pe")

                def evq(e, pb=pb, lq0=lq0, n=n):
                    return e.tensor_copy(big[:, QTT + lq0 * 128: QTT + (lq0 + n) * 128], bank(pb)[:, 0:n * 128])
                ev_last = kb.emit("vector", evq, deps=[last_mm, scores_last], inc="dv")
                qtt_gate[pb] = ev_last
            for qc in range(NQC):
                gqc = half * NQC + qc
                nkt = (gqc + 1) * QCW // 128
                kt0_cross = gqc * QCW // 128
                for ktile in range(nkt):
                    pb = ktile % 2
                    tmm = kb.emit("tensor", lambda e, ktile=ktile, kv=kv, qc=qc, pb=pb: e.matmul(
                        bank(pb, QCW),
                        kt[:, kv * T + ktile * 128: kv * T + (ktile + 1) * 128],
                        big[:, QTT + qc * QCW: QTT + (qc + 1) * QCW],
                        start=True, stop=True),
                        deps=[ev_last, e_gate.get(("x", pb))], inc="pe")
                    scores_last = tmm
                    tex = kb.emit("scalar", lambda e, ktile=ktile, pb=pb: e.activation(
                        big[:, EOFF + ktile * QCW: EOFF + (ktile + 1) * QCW], bank(pb, QCW), AF.Exp, scale=SM_SCALE),
                        deps=[tmm, e_gate.get(("e", ktile))], inc="ac")
                    e_gate[("x", pb)] = tex
                    etok = tex
                    if ktile >= kt0_cross:
                        o_rel = (ktile - kt0_cross) * 128
                        etok = kb.emit("vector", lambda e, ktile=ktile, o_rel=o_rel: e.tensor_tensor(
                            big[:, EOFF + ktile * QCW: EOFF + (ktile + 1) * QCW],
                            big[:, EOFF + ktile * QCW: EOFF + (ktile + 1) * QCW],
                            maskc[:, 384 - o_rel: 384 - o_rel + QCW], ALU.mult), deps=[tex, TAB], inc="dv")
                    e_gate[("m", ktile)] = etok
                for ql in range(QCW // 128):
                    lq = qc * (QCW // 128) + ql
                    gq = half * HQT + lq
                    cb = 4 + ql % 2
                    for ktile in range(gq + 1):
                        tok = kb.emit("tensor", lambda e, ktile=ktile, ql=ql, kv=kv, gq=gq, cb=cb, qc=qc: e.matmul(
                            bank(cb)[:, 0:129],
                            big[:, EOFF + ktile * QCW + ql * 128: EOFF + ktile * QCW + (ql + 1) * 128],
                            vb[:, ktile * HKV * 132 + kv * 132: ktile * HKV * 132 + kv * 132 + 129],
                            start=(ktile == 0), stop=(ktile == gq)),
                            deps=[e_gate.get(("m", ktile))] + ([ctx_gate.get(cb)] if ktile == 0 else []),
                            inc="pe")
                    kb.emit("vector", lambda e, cb=cb, ql=ql: e.reciprocal(rss[:, ql:ql + 1], bank(cb)[:, 128:129]), deps=[tok], inc="dv")
                    tcx = kb.emit("vector", lambda e, cb=cb, ql=ql, lq=lq, head=head: e.tensor_scalar(
                        big[:, lq * 2048 + head * 128: lq * 2048 + (head + 1) * 128],
                        bank(cb)[:, 0:128], rss[:, ql:ql + 1], None, ALU.mult), inc="dv")
                    ctx_gate[cb] = tcx
                    ctx_gate[("c", head, lq)] = tcx
                lastq = qc * (QCW // 128) + (QCW // 128) - 1
                for ktile in range(nkt):
                    e_gate[("e", ktile)] = ctx_gate[("c", head, lastq)]

        kb.block()
        st["ev"] = {}
        wo_tok = [None, None]
        wo_cons = [None, None]
        res_gate = {}
        FSD0 = ("fs0", kb.cnt.get("fs0", 0)); FSD1 = ("fs1", kb.cnt.get("fs1", 0))
        for g in range(NG):
            q0 = g * GQ
            ct_done = []
            for q in range(GQ):
                lq = q0 + q
                for dt in range(16):
                    pb = 6 + (dt // 4) % 2
                    tmm = kb.emit("tensor", lambda e, lq=lq, dt=dt, pb=pb: e.matmul(
                        bank(pb)[:, (dt % 4) * 128:(dt % 4) * 128 + 128],
                        big[:, lq * 2048 + dt * 128: lq * 2048 + (dt + 1) * 128], ident[:, :],
                        start=True, stop=True),
                        deps=[st["ev"].get(pb), ctx_gate.get(("c", HQ - 1, lq)), res_gate.get(6), res_gate.get(7), TAB], inc="pe")
                    if dt % 4 == 3:
                        dt0 = dt - 3

                        def evc(e, pb=pb, dt0=dt0, q=q):
                            return e.tensor_copy(big[:, EOFF + q * 2048 + dt0 * 128: EOFF + q * 2048 + (dt0 + 4) * 128], bank(pb))
                        tv = kb.emit("vector", evc, deps=[tmm, res_gate.get(("cc", q))], inc="dv")
                        st["ev"][pb] = tv
                        ct_done.append(tv)
            xtok = {}
            ff_tok = {}
            for dc in range(2):
                for q in range(GQ):
                    gq = half * HQT + q0 + q

                    def ldx2(e, gq=gq, q=q, dc=dc):
                        return e.dma_start(out=xf[:, q * 1024:(q + 1) * 1024],
                                           in_=x_h.ap()[gq * 128:(gq + 1) * 128, dc * 1024:(dc + 1) * 1024])

                    def ldf(e, gq=gq, q=q, dc=dc):
                        return e.dma_start(out=wstr_f[:, 4096 + q * 1024: 4096 + (q + 1) * 1024],
                                           in_=ffn_h.ap()[gq * 128:(gq + 1) * 128, dc * 1024:(dc + 1) * 1024])
                    xtok[q] = kb.emit("sync", ldx2, deps=(xcons[q % 2], res_gate.get(("xr", q))), inc="x%d" % (q % 2), dma=True)
                    ff_tok[q] = kb.emit("sync", ldf, deps=(ffcons[q % 2], FSD0, FSD1, res_gate.get(("fr", q))) + WGRD(), inc="fb%d" % (q % 2), dma=True)
                for dtb in range(4):
                    s2 = (dc * 4 + dtb) % 2

                    def ldwo(e, dtb=dtb, dc=dc, s2=s2):
                        return e.dma_start(out=wstr[:, s2 * 4096:(s2 + 1) * 4096].rearrange("p (dt c) -> p dt c", dt=4),
                                           in_=wo_h.ap()[dtb * 512:(dtb + 1) * 512, dc * 1024:(dc + 1) * 1024].rearrange("(dt p) c -> p dt c", p=128))
                    wo_tok[s2] = kb.emit("gpsimd", ldwo, deps=(wo_cons[s2],) + WGRD(), inc="wo%d" % s2, dma=True)
                    for dtl in range(4):
                        dt = dtb * 4 + dtl
                        for q in range(GQ):
                            for hb in range(2):
                                tok = kb.emit("tensor", lambda e, dt=dt, dtl=dtl, q=q, hb=hb, s2=s2: e.matmul(
                                    bank(q * 2 + hb),
                                    big[:, EOFF + q * 2048 + dt * 128: EOFF + q * 2048 + (dt + 1) * 128],
                                    wstr[:, s2 * 4096 + dtl * 1024 + hb * 512: s2 * 4096 + dtl * 1024 + (hb + 1) * 512],
                                    start=(dt == 0), stop=(dt == 15)),
                                    deps=[wo_tok[s2]] + (ct_done + [res_gate.get(q * 2 + hb)] if dt == 0 else []),
                                    inc="pe")
                    wo_cons[s2] = tok
                for q in range(GQ):
                    gq = half * HQT + q0 + q
                    s2 = q % 2
                    o2 = outs[:, s2 * 1024:(s2 + 1) * 1024]
                    ta = kb.emit("vector", lambda e, q=q, o2=o2: e.tensor_tensor(
                        o2, ps[:, q * 1024:(q + 1) * 1024],
                        xf[:, q * 1024:(q + 1) * 1024], ALU.add),
                        deps=[tok, xtok[q], outs_gate[s2]], inc="dv")
                    tb2 = kb.emit("vector", lambda e, q=q, o2=o2: e.tensor_tensor(
                        o2, o2, wstr_f[:, 4096 + q * 1024: 4096 + (q + 1) * 1024], ALU.add),
                        deps=[ff_tok[q]], inc="dv")
                    res_gate[q * 2] = tb2
                    res_gate[q * 2 + 1] = tb2
                    res_gate[("xr", q)] = ta
                    res_gate[("fr", q)] = tb2

                    def sto(e, gq=gq, dc=dc, o2=o2):
                        return e.dma_start(out=out_h.ap()[gq * 128:(gq + 1) * 128, dc * 1024:(dc + 1) * 1024], in_=o2)
                    outs_gate[s2] = kb.emit("sync", sto, deps=[tb2], inc="os%d" % s2, dma=True)
                    if dc == 1:
                        xcons[s2] = tb2
                        ffcons[s2] = tb2
                        res_gate[("cc", q)] = tb2

    kb.block()
    kb.emit("sync", lambda e: e.nop(), deps=[("os0", kb.cnt["os0"]), ("os1", kb.cnt.get("os1", 0)), ("fs0", kb.cnt.get("fs0", 0)), ("fs1", kb.cnt.get("fs1", 0)), WSD, ("ws", kb.cnt["ws"])])

    kb.realize()
    return nc


_NC_CACHE = {}


def _get_nc(T, HF):
    key = (T, HF)
    if key not in _NC_CACHE:
        _NC_CACHE[key] = build_nc(T, HF)
    return _NC_CACHE[key]


def _host_tables(T):
    bf = ml_dtypes.bfloat16
    inv = 1.0 / (10000.0 ** (np.arange(0, DH, 2, dtype=np.float64) / DH))
    ang = np.arange(T, dtype=np.float64)[:, None] * inv[None, :]
    cos = np.cos(ang).astype(np.float32)
    sin = np.sin(ang).astype(np.float32)
    costab = np.ascontiguousarray(np.concatenate([cos, cos], axis=1))
    sintab = np.ascontiguousarray(sin)
    ident = np.eye(128, dtype=bf)
    k_i = np.arange(128)[:, None]
    t_i = np.arange(896)[None, :]
    maskc = ((t_i - 384) >= k_i).astype(bf)
    return costab, sintab, ident, maskc


def _deinterleave_cols(w, block=128):
    d_out = w.shape[1]
    perm = []
    for b0 in range(0, d_out, block):
        perm.extend(range(b0, b0 + block, 2))
        perm.extend(range(b0 + 1, b0 + block, 2))
    return np.ascontiguousarray(w[:, perm])


def _permute_wo_rows(wo):
    rows = []
    for h in range(HQ):
        src = 4 * (h % 4) + h // 4
        rows.extend(range(src * 128, (src + 1) * 128))
    return np.ascontiguousarray(wo[rows, :])


def kernel(L0_attn_x, L0_attn_Wq, L0_attn_Wk, L0_attn_Wv, L0_attn_Wo,
           L0_ffn_w_gate_up, L0_ffn_w_proj):
    x = np.ascontiguousarray(np.asarray(L0_attn_x, dtype=np.float32))
    wq = _deinterleave_cols(np.asarray(L0_attn_Wq, dtype=np.float32))
    wk = _deinterleave_cols(np.asarray(L0_attn_Wk, dtype=np.float32))
    wv = np.ascontiguousarray(np.asarray(L0_attn_Wv, dtype=np.float32))
    wo = _permute_wo_rows(np.asarray(L0_attn_Wo, dtype=np.float32))
    wgu = np.ascontiguousarray(np.asarray(L0_ffn_w_gate_up, dtype=np.float32))
    wpj = np.ascontiguousarray(np.asarray(L0_ffn_w_proj, dtype=np.float32))
    B, T, _ = x.shape
    HF = wpj.shape[1]
    costab, sintab, ident, maskc = _host_tables(T)
    nc = _get_nc(T, HF)
    swg, swp = D // 8, HF // 8
    in_maps = [dict(x=x[b], wq=wq, wk=wk, wv=wv, wo=wo,
                    wgu_s=np.ascontiguousarray(wgu[:, b * swg:(b + 1) * swg]),
                    wproj_s=np.ascontiguousarray(wpj[:, b * swp:(b + 1) * swp]),
                    costab=costab, sintab=sintab, ident=ident, maskc=maskc)
               for b in range(B)]
    res = run_bass_kernel_spmd(nc, in_maps, core_ids=list(range(B)))
    return np.stack([np.asarray(res.results[b]["out"]) for b in range(B)], axis=0)
